# version 03
# baseline (speedup 1.0000x reference)
import contextlib
import numpy as np
import concourse.bass as bass
import concourse.mybir as mybir
from concourse.bass_utils import run_bass_kernel_spmd

F32 = mybir.dt.float32
BF16 = mybir.dt.bfloat16
FP8 = mybir.dt.float8e4
AF = mybir.ActivationFunctionType
OP = mybir.AluOpType
AX = mybir.AxisListType

D = 2048
DI = 4096
DH = DI // 2
FF = 8192
NS = 16
TL = 2048
HALO = 3
TP = TL + HALO
SQ = 4096
G_D = D // 128
G_DI = DI // 128
G_DH = DH // 128
G_FF = FF // 128
TCH = [HALO, 512, 512, 512, 512]
TOFF = [0, 3, 515, 1027, 1539]
TCH2 = [512, 512, 512, 512]
TOFF2 = [0, 512, 1024, 1536]
LCH = 512
NCH = SQ // LCH
EPS = 1e-5
FP8MAX = 448.0
PAIRS = [[0, 1], [2, 3], [4, 5], [6, 7]]

DEBUG_OUTS = ()


class Ctx:
    def __init__(self, nc, cs):
        self.nc = nc
        self.cs = cs
        self.ops = {k: [] for k in ("sync", "scalar", "vector", "tensor", "gpsimd")}
        self.sems = {}
        self.cnt = {}

    def sem(self, name):
        if name not in self.sems:
            self.sems[name] = self.cs.enter_context(self.nc.semaphore(name))
            self.cnt[name] = 0
        return self.sems[name]

    def emit(self, eng, fn):
        self.ops[eng].append(fn)

    def inc(self, eng, sem_name, amt, fn):
        s = self.sem(sem_name)
        self.cnt[sem_name] += amt
        tgt = self.cnt[sem_name]

        def run(e, fn=fn, s=s, amt=amt):
            fn(e).then_inc(s, amt)
        self.emit(eng, run)
        return tgt

    def wait(self, eng, sem_name, value):
        if value is None or value <= 0:
            return
        s = self.sem(sem_name)
        assert value <= self.cnt[sem_name], (sem_name, value, self.cnt[sem_name])
        self.emit(eng, lambda e, s=s, v=value: e.wait_ge(s, v))

    def val(self, sem_name):
        self.sem(sem_name)
        return self.cnt[sem_name]

    def dma(self, eng, out, in_, sem_name, slow=False):
        if slow:
            return self.inc(eng, sem_name, 16,
                            lambda e, o=out, i=in_: e.dma_start(
                                out=o, in_=i, allow_slow_non_contiguous=True))
        return self.inc(eng, sem_name, 16,
                        lambda e, o=out, i=in_: e.dma_start(out=o, in_=i))

    def op(self, eng, fn):
        self.emit(eng, fn)

    def barrier_from(self, eng, *sems):
        for sname in sems:
            self.wait(eng, sname, self.val(sname))

    def full_barrier(self):
        names = list(self.sems.keys())
        for eng in ("sync", "scalar", "vector", "tensor", "gpsimd"):
            for sname in names:
                self.wait(eng, sname, self.val(sname))


class Chan:

    _n = [0]

    def __init__(self, ctx):
        self.ctx = ctx
        Chan._n[0] += 1
        self.name = "ch%d" % Chan._n[0]
        ctx.sem(self.name)

    def dma(self, eng, out, in_, slow=False):
        return self.ctx.dma(eng, out, in_, self.name, slow)

    def wait(self, eng, v=None):
        self.ctx.wait(eng, self.name,
                      self.ctx.val(self.name) if v is None else v)


class Rot:

    def __init__(self, ctx, n, chan=True):
        self.ctx = ctx
        self.n = n
        self.g = [[] for _ in range(n)]
        self.ch = [Chan(ctx) if chan else None for _ in range(n)]
        self.i = 0

    def acquire(self, eng):
        i = self.i % self.n
        self.i += 1
        for s, v in self.g[i]:
            self.ctx.wait(eng, s, v)
        self.g[i] = []
        return i

    def load(self, i, eng, out, in_, slow=False):
        return self.ch[i].dma(eng, out, in_, slow)

    def wait_ch(self, i, eng):
        self.ch[i].wait(eng)

    def release(self, i, sem, val):
        self.g[i].append((sem, val))

    def release_ch(self, i):
        self.g[i].append((self.ch[i].name, self.ctx.val(self.ch[i].name)))


def sub_cols(total, cap=512):
    out, o = [], 0
    while o < total:
        w = min(cap, total - o)
        out.append((o, w))
        o += w
    return out


def build_program():
    nc = bass.Bass("TRN2", target_bir_lowering=False, debug=False)
    cs = contextlib.ExitStack()
    cs.__enter__()
    T, S, P = {}, {}, {}

    def dram_in(name, shape, dt):
        T[name] = nc.dram_tensor(name, shape, dt, kind="ExternalInput").ap()

    def dram_out(name, shape, dt):
        T[name] = nc.dram_tensor(name, shape, dt, kind="ExternalOutput").ap()

    def dram_tmp(name, shape, dt):
        kind = "ExternalOutput" if name in DEBUG_OUTS else "Internal"
        T[name] = nc.dram_tensor(name, shape, dt, kind=kind).ap()

    dram_in("x0t", [D, TP], F32)
    dram_in("winT", [D, FF], F32)
    dram_in("wxT", [DI, 33], F32)
    dram_in("outwT", [DI, D], F32)
    dram_in("wfcT", [D, FF], F32)
    dram_in("wprojT", [FF, D], F32)
    dram_in("wfc_sl", [D // 8, FF], F32)
    dram_in("wproj_sl", [FF // 8, D], F32)
    dram_in("a_log_h", [DH, NS], F32)
    dram_in("d_h", [1, DH], F32)
    dram_in("conv_w", [DI, 4], F32)
    dram_in("conv_b", [1, DI], F32)
    dram_in("dt_bias", [1, DI], F32)
    dram_in("flags", [1, 2], F32)
    dram_out("out", [D, TL], F32)

    dram_tmp("xhat1", [D, TP], F32)
    dram_tmp("h1", [FF, TP], F32)
    dram_tmp("wfc8", [D, FF], FP8)
    dram_tmp("wproj8", [FF, D], FP8)
    dram_tmp("win_bf", [D, FF], BF16)
    dram_tmp("x1", [D, TP], F32)
    dram_tmp("xhat2", [D, TP], BF16)
    dram_tmp("ldt_d", [1, SQ], F32)
    for w in range(4):
        dram_tmp("xcv_w%d" % w, [DI, 512], BF16)
        dram_tmp("szv_w%d" % w, [DI, 512], BF16)
        dram_tmp("xcv_e%d" % w, [2 * DI, 512], BF16)
        dram_tmp("szv_e%d" % w, [2 * DI, 512], BF16)
    for w in range(8):
        dram_tmp("yg_w%d" % w, [DH, 512], BF16)
        dram_tmp("yg_e%d" % w, [2 * DH, 512], BF16)
    dram_tmp("p_loc", [33, TL], F32)
    dram_tmp("p_ex", [66, TL], F32)
    dram_tmp("x2", [D, TL], F32)
    dram_tmp("xhat3", [D, TL], F32)
    dram_tmp("h2", [FF, TL], F32)
    for i in (1, 2, 3, 4):
        dram_tmp("ar%d_i" % i, [1, 8], F32)
        dram_tmp("ar%d_o" % i, [1, 8], F32)
    dram_tmp("dbg_sc", [1, 64], F32)
    dram_tmp("dbg_am", [128, 8], F32)
    dram_tmp("fold_d", [128, 1], F32)

    def sb(name, shape, dt):
        S[name] = cs.enter_context(nc.sbuf_tensor(name, shape, dt))
        return S[name]

    def psum(name, shape):
        P[name] = cs.enter_context(nc.psum_tensor(name, shape, F32))
        return P[name]

    sb("onesr", [1, 128], F32)
    sb("halfr", [1, 128], F32)
    sb("onesc", [128, 1], F32)
    sb("apos", [128, G_DH * NS], F32)
    sb("dcol", [128, G_DH], F32)
    sb("cw", [128, G_DI * 4], F32)
    sb("cb", [128, G_DI], F32)
    sb("mdb", [1, 1], F32)
    sb("nmdb", [1, 1], F32)
    for n_ in ("qx1", "qh1", "qx3", "qh2", "qwfc", "qwpj",
               "dq1", "dq2", "dq3", "dq4", "qt0", "qt1",
               "fl0", "fl1", "amax_a", "amax_b"):
        sb(n_, [128, 1], F32)
    for n_ in ("sc1", "sc2", "sc3", "sc4", "gsc1", "gsc2", "gsc3", "gsc4",
               "flrow"):
        sb(n_, [1, 8], F32)
    sb("pfold", [1, 128], F32)
    sb("ln_m", [1, 512], F32)
    sb("ln_v", [1, 512], F32)
    sb("ln_mB", [128, 512], F32)
    sb("ln_iB", [128, 512], F32)
    sb("stA0", [128, 512], F32)
    sb("stA1", [128, 512], F32)
    sb("stB0", [128, 512], F32)
    sb("stB1", [128, 512], F32)
    sb("bf0", [128, 512], BF16)
    sb("bf1", [128, 512], BF16)
    sb("szbf", [128, 512], BF16)
    sb("dtbias_r", [128, DI // 128], F32)
    sb("cvals", [128, 2], F32)

    base = (nc.sbuf_base + 511) // 512 * 512 + 2048

    def overlay(name, shape, dt, off):
        h = nc.alloc_sbuf_tensor_at(name, shape, dt, offset=base + off)
        S[name] = h
        return h

    R0 = 0
    overlay("x0a", [128, G_D * 512], F32, R0)
    overlay("x0b", [128, G_D * 512], F32, R0 + 32768)
    overlay("wpj8", [128, G_FF * D], FP8, R0)
    overlay("owb", [128, G_DI * D], BF16, R0)
    overlay("xsg", [128, G_DI * 515], BF16, R0)
    overlay("xh2c", [128, G_D * 512], BF16, R0 + 33280)
    overlay("slabA", [128, G_D * 128], BF16, R0 + 49664)
    overlay("slabB", [128, G_D * 128], BF16, R0 + 53760)
    overlay("wxb", [128, G_DI * 33], BF16, R0 + 57856)

    R1 = 131072
    overlay("x8", [128, G_D * TP], FP8, R1)
    overlay("hc8", [128, G_FF * 512], FP8, R1)
    overlay("brow", [128, NS * LCH], BF16, R1)
    overlay("crow", [128, NS * LCH], BF16, R1 + NS * LCH * 2)
    overlay("ygc", [128, G_DI * 512], BF16, R1)

    R2 = 164096
    overlay("wf32a", [128, 512], F32, R2)
    overlay("wf32b", [128, 512], F32, R2 + 2048)
    overlay("wf8a", [128, G_D * 128], FP8, R2 + 4096)
    overlay("wf8b", [128, G_D * 128], FP8, R2 + 6144)
    overlay("hrow0", [128, 512], F32, R2 + 8192)
    overlay("hrow1", [128, 512], F32, R2 + 10240)
    overlay("hsq0", [128, 512], F32, R2 + 12288)
    overlay("hsq1", [128, 512], F32, R2 + 14336)
    overlay("hbf0", [128, 512], BF16, R2 + 16384)
    overlay("hbf1", [128, 512], BF16, R2 + 17408)
    overlay("amaxcols", [128, 320], F32, R2 + 18432)
    o = R2
    overlay("ldtb", [128, LCH], F32, o); o += 2048
    overlay("xcg", [128, LCH], BF16, o); o += 1024
    overlay("szg", [128, LCH], BF16, o); o += 1024
    overlay("xcgB", [128, LCH], BF16, o); o += 1024
    overlay("szgB", [128, LCH], BF16, o); o += 1024
    overlay("ug", [128, LCH], BF16, o); o += 1024
    overlay("abar0", [128, LCH], BF16, o); o += 1024
    overlay("abar1", [128, LCH], BF16, o); o += 1024
    overlay("bu", [128, LCH], BF16, o); o += 1024
    overlay("hst0", [128, LCH], BF16, o); o += 1024
    overlay("hst1", [128, LCH], BF16, o); o += 1024
    overlay("prod", [128, LCH], BF16, o); o += 1024
    overlay("yac", [128, LCH], F32, o); o += 2048
    overlay("ygt", [128, LCH], BF16, o); o += 1024
    overlay("carry", [128, G_DH * NS], F32, o); o += 1024
    overlay("prow", [1, 1024], F32, o); o += 4096
    avail = nc.sbuf_top - base
    assert o <= avail and R2 + 18432 + 1280 <= avail, (o, avail)

    psum("pm0", [128, 512])
    psum("pm1", [128, 512])
    psum("pbc", [128, 512])
    psum("pst1", [1, 512])
    psum("pst2", [1, 512])
    psum("pwx", [33, 512])
    psum("psm", [128, 4])

    nc.const_aps.register(F32, EPS, S["cvals"][:, 0:1])
    ctx = Ctx(nc, cs)
    bc_guard = [0]
    psm_guard = [0]
    pfold_guard = [0]

    def bcast_row(row_ap, width, out_ap, twait=()):
        ctx.wait("tensor", "s", bc_guard[0])
        for sname, v in twait:
            ctx.wait("tensor", sname, v)
        tv = ctx.inc("tensor", "t", 1,
                     lambda e, r=row_ap, w=width: e.matmul(
                         P["pbc"][:, 0:w], S["onesr"][:, 0:128], r,
                         start=True, stop=True))
        ctx.wait("scalar", "t", tv)
        sv = ctx.inc("scalar", "s", 1,
                     lambda e, o=out_ap, w=width: e.copy(o, P["pbc"][:, 0:w]))
        bc_guard[0] = sv
        return sv

    def bcast_col(src11, dst_col, twait=()):
        ctx.wait("tensor", "s", psm_guard[0])
        for sname, v in twait:
            ctx.wait("tensor", sname, v)
        tv = ctx.inc("tensor", "t", 1, lambda e, s=src11: e.matmul(
            P["psm"][:, 0:1], S["onesr"][:, 0:128], s, start=True, stop=True))
        ctx.wait("scalar", "t", tv)
        sv = ctx.inc("scalar", "s", 1,
                     lambda e, d=dst_col: e.copy(d, P["psm"][:, 0:1]))
        psm_guard[0] = sv
        return sv

    ch_fold = Chan(ctx)

    def part_fold_max(col_ap, out11_ap, gwait=()):
        for sname, v in gwait:
            ctx.wait("gpsimd", sname, v)
        ctx.wait("gpsimd", "v", pfold_guard[0])
        mid = ch_fold.dma("gpsimd", T["fold_d"][:, :], col_ap)
        ctx.wait("gpsimd", ch_fold.name, mid)
        ch_fold.dma("gpsimd", S["pfold"][0:1, 0:128],
                    T["fold_d"].rearrange("(o f) one -> o (f one)", o=1))
        ch_fold.wait("vector")
        rv = ctx.inc("vector", "v", 1,
                     lambda e, o=out11_ap: e.tensor_reduce(
                         o, S["pfold"][0:1, 0:128], AX.X, OP.max,
                         apply_absolute_value=True))
        pfold_guard[0] = rv
        return rv

    def amax_finish(f0, f1, dst, vlast):
        ctx.wait("gpsimd", "v", vlast)
        return ctx.inc("gpsimd", "g", 1,
                       lambda e, f0=f0, f1=f1, d=dst: e.tensor_reduce(
                           d, S["amaxcols"][:, f0:f1], AX.XYZWC, OP.max))

    def ln_stats_mm(g, n_g, tc, x_ap, sq_ap):
        last = (g == n_g - 1)
        ctx.op("tensor", lambda e, g=g, w=tc, x=x_ap, last=last: e.matmul(
            P["pst1"][:, 0:w], S["onesc"][:, 0:1], x,
            start=(g == 0), stop=last))
        return ctx.inc("tensor", "t", 1,
                       lambda e, g=g, w=tc, x=sq_ap, last=last: e.matmul(
                           P["pst2"][:, 0:w], S["onesc"][:, 0:1], x,
                           start=(g == 0), stop=last))

    def ln_finish(tc, nfeat, tval):
        ctx.wait("scalar", "t", tval)
        ctx.op("scalar", lambda e, w=tc: e.activation(
            S["ln_m"][:, 0:w], P["pst1"][:, 0:w], AF.Copy, scale=1.0 / nfeat))
        ctx.op("scalar", lambda e, w=tc: e.activation(
            S["ln_iB"][0:1, 0:w], S["ln_m"][:, 0:w], AF.Square))
        sv = ctx.inc("scalar", "s", 1, lambda e, w=tc: e.activation(
            S["ln_v"][:, 0:w], P["pst2"][:, 0:w], AF.Copy, scale=1.0 / nfeat))
        ctx.wait("vector", "s", sv)
        vv = ctx.inc("vector", "v", 1, lambda e, w=tc: e.tensor_sub(
            S["ln_v"][:, 0:w], S["ln_v"][:, 0:w], S["ln_iB"][0:1, 0:w]))
        ctx.wait("scalar", "v", vv)
        ctx.op("scalar", lambda e, w=tc: e.activation(
            S["ln_v"][:, 0:w], S["ln_v"][:, 0:w], AF.Ln, bias=EPS))
        sv = ctx.inc("scalar", "s", 1, lambda e, w=tc: e.activation(
            S["ln_v"][:, 0:w], S["ln_v"][:, 0:w], AF.Exp, scale=-0.5))
        bcast_row(S["ln_m"][0:1, 0:tc], tc, S["ln_mB"][:, 0:tc], twait=[("s", sv)])
        return bcast_row(S["ln_v"][0:1, 0:tc], tc, S["ln_iB"][:, 0:tc])

    ch_const = Chan(ctx)
    ch_p1st = Chan(ctx)
    ch_ar = Chan(ctx)
    ch_wpj = Chan(ctx)
    ch_ldt = Chan(ctx)

    def phase_w():
        ctx.op("vector", lambda e: e.memset(S["cvals"][:, 0:1], EPS))
        ctx.op("vector", lambda e: e.memset(S["onesr"][:, :], 1.0))
        ctx.op("vector", lambda e: e.memset(S["halfr"][:, :], 0.5))
        ctx.op("vector", lambda e: e.memset(S["onesc"][:, :], 1.0))
        for n_ in ("sc1", "sc2", "sc3", "sc4"):
            ctx.op("vector", lambda e, n_=n_: e.memset(S[n_][:, :], 0.0))
        ctx.op("vector", lambda e: e.memset(S["amaxcols"][:, :], 0.0))
        ms_v = ctx.inc("vector", "v", 1, lambda e: e.memset(S["carry"][:, :], 0.0))

        ch_const.dma("sync",
                     S["apos"][:, :].rearrange("p (g n) -> p g n", n=NS),
                     T["a_log_h"].rearrange("(g p) n -> p g n", p=128))
        ch_const.dma("sync",
                     S["dcol"][:, :].rearrange("p (o g) -> p o g", o=1),
                     T["d_h"].rearrange("one (g p) -> p one g", p=128), slow=True)
        ch_const.dma("sync",
                     S["cw"][:, :].rearrange("p (g j) -> p g j", j=4),
                     T["conv_w"].rearrange("(g p) j -> p g j", p=128))
        ch_const.dma("sync",
                     S["cb"][:, :].rearrange("p (o g) -> p o g", o=1),
                     T["conv_b"].rearrange("one (g p) -> p one g", p=128), slow=True)
        ch_const.dma("sync",
                     S["dtbias_r"][:, :].rearrange("p (o g) -> p o g", o=1),
                     T["dt_bias"].rearrange("one (p g) -> p one g", p=128), slow=True)
        ch_const.dma("sync", S["flrow"][0:1, 0:2], T["flags"][:, :])
        ch_const.wait("scalar")
        ch_const.wait("vector")
        ch_const.wait("tensor")
        ctx.op("scalar", lambda e: e.activation(S["apos"][:, :], S["apos"][:, :],
                                                AF.Exp))
        ctx.wait("tensor", "v", ms_v)
        bcast_col(S["flrow"][0:1, 0:1], S["fl0"][:, :])
        bcast_col(S["flrow"][0:1, 1:2], S["fl1"][:, :])
        vv = ctx.inc("vector", "v", 1, lambda e: e.tensor_reduce(
            S["qt0"][:, 0:1], S["dtbias_r"][:, :], AX.X, OP.add))
        ctx.wait("tensor", "v", vv)
        ctx.wait("tensor", "s", psm_guard[0])
        tv = ctx.inc("tensor", "t", 1, lambda e: e.matmul(
            P["psm"][0:1, 0:1], S["onesc"][:, 0:1], S["qt0"][:, 0:1],
            start=True, stop=True))
        ctx.wait("scalar", "t", tv)
        ctx.op("scalar", lambda e: e.activation(S["mdb"][:, :], P["psm"][0:1, 0:1],
                                                AF.Copy, scale=1.0 / DI))
        psm_guard[0] = ctx.inc("scalar", "s", 1, lambda e: e.activation(
            S["nmdb"][:, :], S["mdb"][:, :], AF.Copy, scale=-1.0))

        stg = Rot(ctx, 2)
        tiles = ["wf32a", "wf32b"]
        nt = 0
        marks = {}
        for srcd, rows, cols, acc in ((T["wfc_sl"], D // 8, FF, "amax_a"),
                                      (T["wproj_sl"], FF // 8, D, "amax_b")):
            first = nt
            for rt in range(rows // 128):
                for co, cw_ in sub_cols(cols):
                    i = stg.acquire("sync")
                    tile = S[tiles[i]]
                    stg.load(i, "sync", tile[:, 0:cw_],
                             srcd[rt * 128:(rt + 1) * 128, co:co + cw_])
                    stg.wait_ch(i, "vector")
                    vv = ctx.inc("vector", "v", 1,
                                 lambda e, t=tile, w=cw_, j=nt: e.tensor_reduce(
                                     S["amaxcols"][:, j:j + 1], t[:, 0:w], AX.X,
                                     OP.max, apply_absolute_value=True))
                    stg.release(i, "v", vv)
                    nt += 1
            marks[acc] = (first, nt, ctx.val("v"))
        for acc, dst in (("amax_a", S["sc1"][0:1, 1:2]),
                         ("amax_b", S["sc1"][0:1, 2:3])):
            f0, f1, vl = marks[acc]
            amax_finish(f0, f1, dst, vl)

    def phase_p1():
        ctx.full_barrier()
        bufs = ["x0a", "x0b"]
        brot = Rot(ctx, 2)
        bfrot = Rot(ctx, 2, chan=False)
        sqrot = Rot(ctx, 2, chan=False)
        pend = []

        def load_chunk(ci):
            i = brot.acquire("sync")
            for g in range(G_D):
                brot.load(i, "sync",
                          S[bufs[i]][:, g * 512:g * 512 + TCH[ci]],
                          T["x0t"][g * 128:(g + 1) * 128,
                                   TOFF[ci]:TOFF[ci] + TCH[ci]])
            pend.append(i)

        load_chunk(0)
        load_chunk(1)
        for ci, tc in enumerate(TCH):
            slot = pend[ci]
            buf = bufs[slot]
            brot.wait_ch(slot, "scalar")
            brot.wait_ch(slot, "vector")
            brot.wait_ch(slot, "tensor")
            tlast = None
            for g in range(G_D):
                x_ap = S[buf][:, g * 512:g * 512 + tc]
                si = sqrot.acquire("scalar")
                sq = S["wf32a" if si == 0 else "wf32b"]
                sv = ctx.inc("scalar", "s", 1,
                             lambda e, x=x_ap, q=sq, w=tc: e.activation(
                                 q[:, 0:w], x, AF.Square))
                ctx.wait("tensor", "s", sv)
                tlast = ln_stats_mm(g, G_D, tc, x_ap, sq[:, 0:tc])
                sqrot.release(si, "t", tlast)
            bv = ln_finish(tc, D, tlast)
            ctx.wait("vector", "s", bv)
            for g in range(G_D):
                x_ap = S[buf][:, g * 512:g * 512 + 512]
                bj = bfrot.acquire("vector")
                xt = S["stB0" if bj == 0 else "stA1"]
                ctx.op("vector", lambda e, x=x_ap: e.tensor_sub(
                    S["stA0"][:, 0:512], x, S["ln_mB"][:, 0:512]))
                ctx.op("vector", lambda e, xt=xt: e.tensor_mul(
                    xt[:, 0:512], S["stA0"][:, 0:512], S["ln_iB"][:, 0:512]))
                vlast = ctx.inc("vector", "v", 1,
                                lambda e, w=tc, xt=xt,
                                j=ci * G_D + g: e.tensor_reduce(
                                    S["amaxcols"][:, j:j + 1], xt[:, 0:w],
                                    AX.X, OP.max, apply_absolute_value=True))
                ctx.wait("scalar", "v", vlast)
                wv = ch_p1st.dma("scalar",
                                 T["xhat1"][g * 128:(g + 1) * 128,
                                            TOFF[ci]:TOFF[ci] + tc],
                                 xt[:, 0:tc])
                bfrot.release(bj, ch_p1st.name, wv)
            brot.release(slot, "v", ctx.val("v"))
            if ci + 2 < len(TCH):
                load_chunk(ci + 2)
        return amax_finish(0, len(TCH) * G_D, S["sc1"][0:1, 0:1], ctx.val("v"))

    def allreduce_max(tag):
        ctx.barrier_from("gpsimd", "v", "s")
        i_d, o_d = T["ar%d_i" % tag], T["ar%d_o" % tag]
        ch_ar.dma("gpsimd", i_d[:, :], S["sc%d" % tag][:, :])
        ch_ar.wait("gpsimd")
        cv = ctx.inc("gpsimd", "cc", 1,
                     lambda e, i=i_d, o=o_d: e.collective_compute(
                         "AllReduce", OP.max, replica_groups=[list(range(8))],
                         ins=[i.opt()], outs=[o.opt()]))
        ctx.wait("gpsimd", "cc", cv)
        ch_ar.dma("gpsimd", S["gsc%d" % tag][:, :], o_d[:, :])
        ch_ar.wait("scalar")
        ch_ar.wait("vector")
        ch_ar.wait("tensor")

    def mk_s_col(src11, dst_col, k=1.0):
        bv = bcast_col(src11, dst_col)
        sv = ctx.inc("scalar", "s", 1, lambda e, d=dst_col, k=k: e.activation(
            d, d, AF.Copy, scale=k / FP8MAX))
        ctx.wait("vector", "s", sv)
        return ctx.inc("vector", "v", 1,
                       lambda e, d=dst_col, k=k: e.tensor_scalar_max(
                           d, d, 1e-6 * k))

    def mk_scale(src11, qdst):
        vv = mk_s_col(src11, S[qdst][:, :])
        ctx.wait("scalar", "v", vv)
        ctx.op("scalar", lambda e: e.activation(S[qdst][:, :], S[qdst][:, :],
                                                AF.Ln))
        ctx.op("scalar", lambda e: e.activation(
            S[qdst][:, :], S[qdst][:, :], AF.Exp, scale=-1.0))
        return ctx.inc("scalar", "s", 1, lambda e: e.activation(
            S[qdst][:, :], S[qdst][:, :], AF.Copy, scale=0.5))

    def mk_dq(srcx11, srcw11, basef, dqdst):
        mk_s_col(srcx11, S["qt0"][:, :], k=4.0 * basef)
        vv = mk_s_col(srcw11, S["qt1"][:, :])
        ctx.wait("gpsimd", "v", vv)
        gv = ctx.inc("gpsimd", "g", 1, lambda e: e.tensor_mul(
            S[dqdst][:, :], S["qt0"][:, :], S["qt1"][:, :]))
        for eng in ("tensor", "scalar", "vector", "sync"):
            ctx.wait(eng, "g", gv)
        return gv

    def phase_fc(xhat_d, h_d, tp, toffs, tchs, quantize_weights, qx, dq, sc_dst):
        ctx.full_barrier()
        frot = Rot(ctx, 2)
        ftiles = ["wf32a", "wf32b"]
        for g in range(G_D):
            for ci, tc in enumerate(tchs):
                i = frot.acquire("sync")
                wf = S[ftiles[i]]
                frot.load(i, "sync", wf[:, 0:tc],
                          xhat_d[g * 128:(g + 1) * 128,
                                 toffs[ci]:toffs[ci] + tc])
                frot.wait_ch(i, "vector")
                vv = ctx.inc("vector", "v", 1,
                             lambda e, g=g, o=toffs[ci], w=tc, wf=wf: e.tensor_scalar_mul(
                                 S["x8"][:, g * tp + o:g * tp + o + w],
                                 wf[:, 0:w], S[qx][:, 0:1]))
                frot.release(i, "v", vv)
        ctx.wait("tensor", "v", ctx.val("v"))

        tsubs = []
        for ci, tc in enumerate(tchs):
            for o2, w2 in sub_cols(tc):
                tsubs.append((toffs[ci] + o2, w2))

        wrot = Rot(ctx, 2)
        pmrot = Rot(ctx, 2, chan=False)
        hrot = Rot(ctx, 2)
        rrot = Rot(ctx, 2, chan=False)
        wready = {}
        wslot = {}

        def load_of(of):
            i = wrot.acquire("sync" if not quantize_weights else "scalar")
            w8 = S["wf8a" if i == 0 else "wf8b"]
            if quantize_weights:
                vv = None
                for k4 in range(G_D // 4):
                    j = frot.acquire("sync")
                    wf = S[ftiles[j]]
                    frot.load(j, "sync",
                              wf[:, 0:512].rearrange("p (kk f) -> p kk f", f=128),
                              T["wfcT"][k4 * 512:(k4 + 1) * 512,
                                        of * 128:(of + 1) * 128]
                              .rearrange("(kk p) f -> p kk f", p=128))
                    frot.wait_ch(j, "vector")
                    vv = ctx.inc("vector", "v", 1,
                                 lambda e, w8=w8, k4=k4, wf=wf: e.tensor_scalar_mul(
                                     w8[:, k4 * 512:(k4 + 1) * 512], wf[:, 0:512],
                                     S["qwfc"][:, 0:1]))
                    frot.release(j, "v", vv)
                ctx.wait("scalar", "v", vv)
                wrot.load(i, "scalar",
                          T["wfc8"][0:D, of * 128:(of + 1) * 128]
                          .rearrange("(kk p) f -> p kk f", p=128),
                          w8[:, 0:G_D * 128].rearrange("p (kk f) -> p kk f", f=128))
                wready[of] = ("v", vv)
            else:
                wrot.load(i, "sync",
                          w8[:, 0:G_D * 128].rearrange("p (kk f) -> p kk f", f=128),
                          T["wfc8"][0:D, of * 128:(of + 1) * 128]
                          .rearrange("(kk p) f -> p kk f", p=128))
                wready[of] = (wrot.ch[i].name, ctx.val(wrot.ch[i].name))
            wslot[of] = i
            return i

        load_of(0)
        for of in range(G_FF):
            if of + 1 < G_FF:
                load_of(of + 1)
            i = wslot[of]
            w8 = S["wf8a" if i == 0 else "wf8b"]
            ctx.wait("tensor", *wready[of])
            for si, (o2, w2) in enumerate(tsubs):
                pj = pmrot.acquire("tensor")
                pm = P["pm0" if pj == 0 else "pm1"]
                tv = None
                for ki in range(G_D):
                    last = ki == G_D - 1
                    fn = lambda e, pm=pm, w8=w8, ki=ki, o2=o2, w2=w2, last=last: e.matmul(
                        pm[:, 0:w2], w8[:, ki * 128:(ki + 1) * 128],
                        S["x8"][:, ki * tp + o2:ki * tp + o2 + w2],
                        start=(ki == 0), stop=last)
                    if last:
                        tv = ctx.inc("tensor", "t", 1, fn)
                    else:
                        ctx.op("tensor", fn)
                ctx.wait("scalar", "t", tv)
                rj = rrot.acquire("scalar")
                hr = S["hrow0" if rj == 0 else "hrow1"]
                sv = ctx.inc("scalar", "s", 1,
                             lambda e, pm=pm, w2=w2, hr=hr: e.activation(
                                 hr[:, 0:w2], pm[:, 0:w2], AF.Relu,
                                 scale=S[dq][:, 0:1]))
                pmrot.release(pj, "s", sv)
                hj = hrot.acquire("vector")
                hb = S["hsq0" if hj == 0 else "hsq1"]
                ctx.wait("vector", "s", sv)
                ctx.op("vector", lambda e, hb=hb, w2=w2, hr=hr: e.tensor_mul(
                    hb[:, 0:w2], hr[:, 0:w2], hr[:, 0:w2]))
                vv = ctx.inc("vector", "v", 1,
                             lambda e, w2=w2, hr=hr,
                             j=of * len(tsubs) + si: e.tensor_reduce(
                                 S["amaxcols"][:, j:j + 1], hr[:, 0:w2], AX.X,
                                 OP.max, apply_absolute_value=True))
                rrot.release(rj, "v", vv)
                ctx.wait("scalar", "v", vv)
                hrot.load(hj, "scalar", h_d[of * 128:(of + 1) * 128, o2:o2 + w2],
                          hb[:, 0:w2])
                hrot.release_ch(hj)
            wrot.release(i, "t", ctx.val("t"))
        cell = S[sc_dst][0:1, 0:1]
        rv = amax_finish(0, G_FF * len(tsubs), cell, ctx.val("v"))
        ctx.wait("scalar", "g", rv)
        ctx.inc("scalar", "s", 1, lambda e, c=cell: e.activation(
            c, c, AF.Square))

    def load_wproj(quantize):
        ctx.full_barrier()
        frot = Rot(ctx, 2)
        lastv = None
        for ki in range(G_FF):
            if quantize:
                for d4 in range(G_D // 4):
                    j = frot.acquire("sync")
                    wf = S["wf32a" if j == 0 else "wf32b"]
                    frot.load(j, "sync", wf[:, 0:512],
                              T["wprojT"][ki * 128:(ki + 1) * 128,
                                          d4 * 512:(d4 + 1) * 512])
                    frot.wait_ch(j, "vector")
                    lastv = ctx.inc("vector", "v", 1,
                                    lambda e, ki=ki, d4=d4, wf=wf: e.tensor_scalar_mul(
                                        S["wpj8"][:, ki * D + d4 * 512:
                                                  ki * D + (d4 + 1) * 512],
                                        wf[:, 0:512], S["qwpj"][:, 0:1]))
                    frot.release(j, "v", lastv)
            else:
                ch_wpj.dma("sync", S["wpj8"][:, ki * D:(ki + 1) * D],
                           T["wproj8"][ki * 128:(ki + 1) * 128, 0:D])
        if quantize:
            ctx.wait("scalar", "v", lastv)
            for ki in range(G_FF):
                ch_wpj.dma("scalar", T["wproj8"][ki * 128:(ki + 1) * 128, 0:D],
                           S["wpj8"][:, ki * D:(ki + 1) * D])
            return ("v", lastv)
        return (ch_wpj.name, ctx.val(ch_wpj.name))

    def phase_proj(h_d, base_d, out_d, toffs, tchs, qh, dq, wready,
                   ln_xhat=None, amax_dst=None, base_off=0):
        ctx.full_barrier()
        ctx.wait("tensor", *wready)
        pmrot = Rot(ctx, 2, chan=False)
        strot = Rot(ctx, 2)
        bfrot = Rot(ctx, 2)
        ch_out = Chan(ctx)
        stg = Rot(ctx, 2)
        for ci, tc in enumerate(tchs):
            t0 = toffs[ci]
            ctx.barrier_from("sync", "t")
            for ki in range(G_FF):
                i = stg.acquire("sync")
                wf = S["wf32a" if i == 0 else "wf32b"]
                stg.load(i, "sync", wf[:, 0:tc],
                         h_d[ki * 128:(ki + 1) * 128, t0:t0 + tc])
                stg.wait_ch(i, "vector")
                vv = ctx.inc("vector", "v", 1,
                             lambda e, ki=ki, w=tc, wf=wf: e.tensor_scalar_mul(
                                 S["hc8"][:, ki * 512:ki * 512 + w],
                                 wf[:, 0:w], S[qh][:, 0:1]))
                stg.release(i, "v", vv)
            ctx.wait("tensor", "v", ctx.val("v"))
            tlast_stats = None
            for dm in range(G_D):
                pj = pmrot.acquire("tensor")
                pm = P["pm0" if pj == 0 else "pm1"]
                tv = None
                for ki in range(G_FF):
                    last = ki == G_FF - 1
                    fn = lambda e, pm=pm, ki=ki, dm=dm, w=tc, last=last: e.matmul(
                        pm[:, 0:w],
                        S["wpj8"][:, ki * D + dm * 128:ki * D + (dm + 1) * 128],
                        S["hc8"][:, ki * 512:ki * 512 + w],
                        start=(ki == 0), stop=last)
                    if last:
                        tv = ctx.inc("tensor", "t", 1, fn)
                    else:
                        ctx.op("tensor", fn)
                si = strot.acquire("sync")
                sta = S["stA0" if si == 0 else "stA1"]
                stb = S["stB0" if si == 0 else "stB1"]
                strot.load(si, "sync", stb[:, 0:tc],
                           base_d[dm * 128:(dm + 1) * 128,
                                  base_off + t0:base_off + t0 + tc])
                ctx.wait("vector", "t", tv)
                strot.wait_ch(si, "vector")
                vv = ctx.inc("vector", "v", 1,
                             lambda e, pm=pm, w=tc, sta=sta, stb=stb: e.scalar_tensor_tensor(
                                 sta[:, 0:w], pm[:, 0:w], S[dq][:, 0:1],
                                 stb[:, 0:w], OP.mult, OP.add))
                pmrot.release(pj, "v", vv)
                ctx.wait("scalar", "v", vv)
                ch_out.dma("scalar", out_d[dm * 128:(dm + 1) * 128, t0:t0 + tc],
                           sta[:, 0:tc])
                if ln_xhat is not None:
                    sv = ctx.inc("scalar", "s", 1,
                                 lambda e, sta=sta, w=tc: e.activation(
                                     S["hsq0"][:, 0:w], sta[:, 0:w], AF.Square))
                    ctx.wait("tensor", "s", sv)
                    tlast_stats = ln_stats_mm(dm, G_D, tc, sta[:, 0:tc],
                                              S["hsq0"][:, 0:tc])
                    strot.release(si, "t", tlast_stats)
                strot.release(si, ch_out.name, ctx.val(ch_out.name))
            if ln_xhat is not None:
                ch_out.wait("sync")
                bv = ln_finish(tc, D, tlast_stats)
                ctx.wait("vector", "s", bv)
                for dm in range(G_D):
                    si = strot.acquire("sync")
                    sta = S["stA0" if si == 0 else "stA1"]
                    strot.load(si, "sync", sta[:, 0:tc],
                               out_d[dm * 128:(dm + 1) * 128, t0:t0 + tc])
                    strot.wait_ch(si, "vector")
                    bj = bfrot.acquire("vector")
                    bft = S["hbf0" if bj == 0 else "hbf1"]
                    ctx.op("vector", lambda e, sta=sta: e.tensor_sub(
                        sta[:, 0:512], sta[:, 0:512], S["ln_mB"][:, 0:512]))
                    ctx.op("vector", lambda e, sta=sta: e.tensor_mul(
                        sta[:, 0:512], sta[:, 0:512], S["ln_iB"][:, 0:512]))
                    vv = ctx.inc("vector", "v", 1,
                                 lambda e, sta=sta, b=bft: e.tensor_copy(
                                     b[:, 0:512], sta[:, 0:512]))
                    if amax_dst is not None:
                        vv = ctx.inc("vector", "v", 1,
                                     lambda e, sta=sta, w=tc,
                                     j=ci * G_D + dm: e.tensor_reduce(
                                         S["amaxcols"][:, j:j + 1], sta[:, 0:w],
                                         AX.X, OP.max,
                                         apply_absolute_value=True))
                    strot.release(si, "v", vv)
                    ctx.wait("scalar", "v", vv)
                    bfrot.load(bj, "scalar",
                               ln_xhat[dm * 128:(dm + 1) * 128, t0:t0 + tc],
                               bft[:, 0:tc])
                    bfrot.release_ch(bj)
        if amax_dst is not None:
            return amax_finish(0, len(tchs) * G_D, S[amax_dst][0:1, 0:1],
                               ctx.val("v"))

    def prep_win_bf():
        ctx.full_barrier()
        frot = Rot(ctx, 2)
        brot = Rot(ctx, 2)
        for ki in range(G_D):
            for co, cw_ in sub_cols(FF):
                j = frot.acquire("sync")
                wf = S["wf32a" if j == 0 else "wf32b"]
                frot.load(j, "sync", wf[:, 0:cw_],
                          T["winT"][ki * 128:(ki + 1) * 128, co:co + cw_])
                i = brot.acquire("scalar")
                bft = S["bf0" if i == 0 else "bf1"]
                frot.wait_ch(j, "scalar")
                sv = ctx.inc("scalar", "s", 1,
                             lambda e, wf=wf, b=bft, w=cw_: e.copy(
                                 b[:, 0:w], wf[:, 0:w]))
                frot.release(j, "s", sv)
                brot.load(i, "scalar",
                          T["win_bf"][ki * 128:(ki + 1) * 128, co:co + cw_],
                          bft[:, 0:cw_])
                brot.release_ch(i)
        sv = None
        for gi in range(G_DI):
            j = frot.acquire("sync")
            wf = S["wf32a" if j == 0 else "wf32b"]
            frot.load(j, "sync", wf[:, 0:33],
                      T["wxT"][gi * 128:(gi + 1) * 128, 0:33])
            frot.wait_ch(j, "scalar")
            sv = ctx.inc("scalar", "s", 1, lambda e, gi=gi, wf=wf: e.copy(
                S["wxb"][:, gi * 33:(gi + 1) * 33], wf[:, 0:33]))
            frot.release(j, "s", sv)
        return sv

    def phase_p4(wx_ready):
        ctx.full_barrier()
        ctx.wait("tensor", "s", wx_ready)
        a2a = []
        slabs = ["slabA", "slabB"]
        srot = Rot(ctx, 2)
        pmrot = Rot(ctx, 2, chan=False)
        ch_x = Chan(ctx)
        ch_xc = Chan(ctx)
        ch_sz = Chan(ctx)
        ch_p = Chan(ctx)
        xrot = Rot(ctx, 2, chan=False)
        crot = Rot(ctx, 2, chan=False)

        def load_slab(of):
            i = srot.acquire("sync")
            sl = S[slabs[i]]
            srot.load(i, "sync",
                      sl[:, 0:G_D * 128].rearrange("p (kk f) -> p kk f", f=128),
                      T["win_bf"][0:D, of * 128:(of + 1) * 128]
                      .rearrange("(kk p) f -> p kk f", p=128))
            return i

        for g in range(G_D):
            ch_x.dma("sync", S["xh2c"][:, g * 512:g * 512 + HALO],
                     T["xhat2"][g * 128:(g + 1) * 128, 0:HALO])
        sl_i = load_slab(0)
        ch_x.wait("tensor")
        for of in range(G_DI):
            nxt = load_slab(of + 1) if of + 1 < G_DI else None
            sl = S[slabs[sl_i]]
            srot.wait_ch(sl_i, "tensor")
            pj = pmrot.acquire("tensor")
            pm = P["pm0" if pj == 0 else "pm1"]
            tv = None
            for ki in range(G_D):
                last = ki == G_D - 1
                fn = lambda e, pm=pm, sl=sl, ki=ki, last=last: e.matmul(
                    pm[:, 0:HALO], sl[:, ki * 128:(ki + 1) * 128],
                    S["xh2c"][:, ki * 512:ki * 512 + HALO],
                    start=(ki == 0), stop=last)
                if last:
                    tv = ctx.inc("tensor", "t", 1, fn)
                else:
                    ctx.op("tensor", fn)
            ctx.wait("scalar", "t", tv)
            sv = ctx.inc("scalar", "s", 1, lambda e, pm=pm, of=of: e.copy(
                S["xsg"][:, of * 515:of * 515 + HALO], pm[:, 0:HALO]))
            pmrot.release(pj, "s", sv)
            srot.release(sl_i, "t", ctx.val("t"))
            if nxt is not None:
                sl_i = nxt

        pv_guard = 0
        for ci in range(4):
            t0 = HALO + ci * 512
            ctx.barrier_from("sync", "t")
            ctx.barrier_from("scalar", "v")
            for g in range(G_D):
                ch_x.dma("sync", S["xh2c"][:, g * 512:(g + 1) * 512],
                         T["xhat2"][g * 128:(g + 1) * 128, t0:t0 + 512])
            ch_x.wait("tensor")
            sl_i = load_slab(0)
            for of in range(2 * G_DI):
                nxt = load_slab(of + 1) if of + 1 < 2 * G_DI else None
                sl = S[slabs[sl_i]]
                srot.wait_ch(sl_i, "tensor")
                pj = pmrot.acquire("tensor")
                pm = P["pm0" if pj == 0 else "pm1"]
                tv = None
                for ki in range(G_D):
                    last = ki == G_D - 1
                    fn = lambda e, pm=pm, sl=sl, ki=ki, last=last: e.matmul(
                        pm[:, 0:512], sl[:, ki * 128:(ki + 1) * 128],
                        S["xh2c"][:, ki * 512:(ki + 1) * 512],
                        start=(ki == 0), stop=last)
                    if last:
                        tv = ctx.inc("tensor", "t", 1, fn)
                    else:
                        ctx.op("tensor", fn)
                ctx.wait("scalar", "t", tv)
                if of < G_DI:
                    sv = ctx.inc("scalar", "s", 1, lambda e, pm=pm, of=of: e.copy(
                        S["xsg"][:, of * 515 + 3:of * 515 + 515], pm[:, 0:512]))
                else:
                    g = of - G_DI
                    ch_sz.wait("scalar")
                    sv = ctx.inc("scalar", "s", 1, lambda e, pm=pm: e.activation(
                        S["szbf"][:, 0:512], pm[:, 0:512], AF.Silu))
                    ch_sz.dma("scalar",
                              T["szv_w%d" % ci][g * 128:(g + 1) * 128, 0:512],
                              S["szbf"][:, 0:512])
                pmrot.release(pj, "s", sv)
                srot.release(sl_i, "t", ctx.val("t"))
                if nxt is not None:
                    sl_i = nxt
            sconv = ctx.val("s")
            ctx.wait("tensor", "s", pv_guard)
            for g in range(G_DI):
                cj = crot.acquire("vector")
                cacc = S["stA0" if cj == 0 else "stA1"]
                if g == 0:
                    ctx.wait("vector", "s", sconv)
                b = g * 515
                ctx.op("vector", lambda e, b=b, g=g, c=cacc: e.tensor_scalar_mul(
                    c[:, 0:512], S["xsg"][:, b:b + 512], S["cw"][:, g * 4:g * 4 + 1]))
                for j in (1, 2, 3):
                    ctx.op("vector", lambda e, b=b, g=g, j=j, c=cacc: e.scalar_tensor_tensor(
                        c[:, 0:512], S["xsg"][:, b + j:b + j + 512],
                        S["cw"][:, g * 4 + j:g * 4 + j + 1],
                        c[:, 0:512], OP.mult, OP.add))
                if ci < 3:
                    ctx.op("vector", lambda e, b=b: e.tensor_copy(
                        S["xsg"][:, b:b + 3], S["xsg"][:, b + 512:b + 515]))
                vv = ctx.inc("vector", "v", 1, lambda e: e.engine_nop())
                ctx.wait("scalar", "v", vv)
                xj = xrot.acquire("scalar")
                xcb = S["bf0" if xj == 0 else "bf1"]
                sv = ctx.inc("scalar", "s", 1,
                             lambda e, g=g, c=cacc, x=xcb: e.activation(
                                 x[:, 0:512], c[:, 0:512], AF.Silu,
                                 bias=S["cb"][:, g:g + 1]))
                crot.release(cj, "s", sv)
                ctx.wait("tensor", "s", sv)
                tvx = ctx.inc("tensor", "t", 1, lambda e, g=g, x=xcb: e.matmul(
                    P["pwx"][:, 0:512], S["wxb"][:, g * 33:(g + 1) * 33],
                    x[:, 0:512], start=(g == 0), stop=(g == G_DI - 1)))
                ctx.wait("scalar", "t", tvx)
                ch_xc.dma("scalar", T["xcv_w%d" % ci][g * 128:(g + 1) * 128, 0:512],
                          xcb[:, 0:512])
                xrot.release(xj, ch_xc.name, ctx.val(ch_xc.name))
            sv = ctx.inc("scalar", "s", 1, lambda e: e.copy(
                S["stB0"][0:33, 0:512], P["pwx"][:, 0:512]))
            pv_guard = sv
            ctx.wait("scalar", "s", sv)
            ch_p.dma("scalar", T["p_loc"][0:33, ci * 512:(ci + 1) * 512],
                     S["stB0"][0:33, 0:512])
            ch_xc.wait("gpsimd")
            ch_sz.wait("gpsimd")
            for nm, exd in (("xcv", "xcv_e"), ("szv", "szv_e")):
                a2a.append(ctx.inc("gpsimd", "cc", 1,
                                   lambda e, nm=nm, exd=exd, ci=ci: e.collective_compute(
                                       "AllGather", OP.bypass, replica_groups=PAIRS,
                                       ins=[T["%s_w%d" % (nm, ci)].opt()],
                                       outs=[T["%s%d" % (exd, ci)].opt()])))
        ch_p.wait("gpsimd")
        pg = ctx.inc("gpsimd", "cc", 1, lambda e: e.collective_compute(
            "AllGather", OP.bypass, replica_groups=PAIRS,
            ins=[T["p_loc"].opt()], outs=[T["p_ex"].opt()]))
        return a2a, pg

    def prep_ldt(pg):
        ctx.wait("sync", "cc", pg)
        rrot = Rot(ctx, 2)
        for j in range(NCH):
            blk, coff = (j * LCH) // TL, (j * LCH) % TL
            i = rrot.acquire("sync")
            rstg = S["prow"][0:1, i * 512:(i + 1) * 512]
            rrot.load(i, "sync", rstg,
                      T["p_ex"][blk * 33 + 32:blk * 33 + 33, coff:coff + LCH])
            rrot.wait_ch(i, "scalar")
            ctx.op("scalar", lambda e, r=rstg: e.activation(
                r, r, AF.Sigmoid, scale=-1.0, bias=S["nmdb"][0:1, 0:1]))
            sv = ctx.inc("scalar", "s", 1, lambda e, r=rstg: e.activation(
                r, r, AF.Ln))
            ctx.wait("scalar", "s", sv)
            ch_ldt.dma("scalar", T["ldt_d"][0:1, j * LCH:(j + 1) * LCH], rstg)
            rrot.release_ch(i)
            rrot.release(i, "s", sv)
        return ctx.val(ch_ldt.name)

    def phase_p5(a2a, ldt_done):
        ctx.wait("sync", ch_ldt.name, ldt_done)
        yga2a = []
        prot = Rot(ctx, 2)
        hrot = Rot(ctx, 2, chan=False)
        arot = Rot(ctx, 2, chan=False)
        ch_g = Chan(ctx)
        ch_yg = Chan(ctx)
        for c in range(NCH):
            blk, coff = (c * LCH) // TL, (c * LCH) % TL
            w = c % 4
            pi = prot.acquire("sync")
            rstg = S["prow"][0:1, pi * 512:(pi + 1) * 512]
            prot.load(pi, "sync", rstg, T["ldt_d"][0:1, c * LCH:(c + 1) * LCH])
            bcast_row(rstg, LCH, S["ldtb"][:, :],
                      twait=[(prot.ch[pi].name, ctx.val(prot.ch[pi].name))])
            prot.release(pi, "t", ctx.val("t"))
            for n in range(NS):
                for nm, dst in (("b", "brow"), ("c", "crow")):
                    row = blk * 33 + (n if nm == "b" else NS + n)
                    pi = prot.acquire("sync")
                    rstg = S["prow"][0:1, pi * 512:(pi + 1) * 512]
                    prot.load(pi, "sync", rstg,
                              T["p_ex"][row:row + 1, coff:coff + LCH])
                    bcast_row(rstg, LCH, S[dst][:, n * LCH:(n + 1) * LCH],
                              twait=[(prot.ch[pi].name,
                                      ctx.val(prot.ch[pi].name))])
                    prot.release(pi, "t", ctx.val("t"))
            bcv = ctx.val("s")
            ctx.wait("sync", "cc", a2a[2 * w + 1])
            for g in range(G_DH):
                ctx.barrier_from("sync", "v")
                rlo = blk * DI + g * 128
                rhi = blk * DI + DH + g * 128
                ch_g.dma("sync", S["xcg"][:, :],
                         T["xcv_e%d" % w][rlo:rlo + 128, 0:LCH])
                ch_g.dma("sync", S["xcgB"][:, :],
                         T["xcv_e%d" % w][rhi:rhi + 128, 0:LCH])
                ch_g.dma("sync", S["szg"][:, :],
                         T["szv_e%d" % w][rlo:rlo + 128, 0:LCH])
                ch_g.dma("sync", S["szgB"][:, :],
                         T["szv_e%d" % w][rhi:rhi + 128, 0:LCH])
                ch_g.wait("vector")
                ctx.wait("vector", "s", bcv)
                ctx.op("vector", lambda e: e.tensor_scalar_mul(
                    S["xcg"][:, :], S["xcg"][:, :], S["fl0"][:, 0:1]))
                ctx.op("vector", lambda e: e.scalar_tensor_tensor(
                    S["xcg"][:, :], S["xcgB"][:, :], S["fl1"][:, 0:1],
                    S["xcg"][:, :], OP.mult, OP.add))
                ctx.op("vector", lambda e: e.tensor_scalar_mul(
                    S["szg"][:, :], S["szg"][:, :], S["fl0"][:, 0:1]))
                ctx.op("vector", lambda e: e.scalar_tensor_tensor(
                    S["szg"][:, :], S["szgB"][:, :], S["fl1"][:, 0:1],
                    S["szg"][:, :], OP.mult, OP.add))
                ctx.op("vector", lambda e: e.scalar_tensor_tensor(
                    S["ug"][:, :], S["xcg"][:, :], -1.0, S["ldtb"][:, :],
                    OP.mult, OP.mult))
                for n in range(NS):
                    aj = arot.acquire("scalar")
                    ab = S["abar0" if aj == 0 else "abar1"]
                    sv = ctx.inc("scalar", "s", 1,
                                 lambda e, g=g, n=n, ab=ab: e.activation(
                                     ab[:, :], S["ldtb"][:, :], AF.Exp,
                                     scale=S["apos"][:, g * NS + n:g * NS + n + 1]))
                    ctx.op("vector", lambda e, n=n: e.tensor_mul(
                        S["bu"][:, :], S["ug"][:, :],
                        S["brow"][:, n * LCH:(n + 1) * LCH]))
                    hj = hrot.acquire("vector")
                    hb = S["hst0" if hj == 0 else "hst1"]
                    ctx.wait("vector", "s", sv)
                    if c == 0:
                        ctx.op("vector", lambda e, hb=hb, ab=ab: e.tensor_tensor_scan(
                            hb[:, :], ab[:, :], S["bu"][:, :], 0.0,
                            OP.mult, OP.add))
                    else:
                        ctx.op("vector", lambda e, hb=hb, ab=ab, g=g, n=n: e.tensor_tensor_scan(
                            hb[:, :], ab[:, :], S["bu"][:, :],
                            S["carry"][:, g * NS + n:g * NS + n + 1],
                            OP.mult, OP.add))
                    vv = ctx.inc("vector", "v", 1, lambda e: e.engine_nop())
                    arot.release(aj, "v", vv)
                    if n == 0:
                        ctx.op("vector", lambda e, hb=hb, n=n: e.tensor_mul(
                            S["yac"][:, :], hb[:, :],
                            S["crow"][:, n * LCH:(n + 1) * LCH]))
                    else:
                        ctx.op("vector", lambda e, hb=hb, n=n: e.tensor_mul(
                            S["prod"][:, :], hb[:, :],
                            S["crow"][:, n * LCH:(n + 1) * LCH]))
                        ctx.op("vector", lambda e: e.tensor_add(
                            S["yac"][:, :], S["yac"][:, :], S["prod"][:, :]))
                    vv2 = ctx.inc("vector", "v", 1, lambda e: e.engine_nop())
                    ctx.wait("gpsimd", "v", vv2)
                    gv = ctx.inc("gpsimd", "g", 1,
                                 lambda e, hb=hb, g=g, n=n: e.tensor_copy(
                                     S["carry"][:, g * NS + n:g * NS + n + 1],
                                     hb[:, LCH - 1:LCH]))
                    hrot.release(hj, "g", gv)
                ctx.op("vector", lambda e, g=g: e.scalar_tensor_tensor(
                    S["prod"][:, :], S["xcg"][:, :], S["dcol"][:, g:g + 1],
                    S["yac"][:, :], OP.mult, OP.add))
                ch_yg.wait("vector")
                vv = ctx.inc("vector", "v", 1, lambda e: e.tensor_mul(
                    S["ygt"][:, :], S["prod"][:, :], S["szg"][:, :]))
                ctx.wait("scalar", "v", vv)
                ch_yg.dma("scalar", T["yg_w%d" % c][g * 128:(g + 1) * 128, 0:LCH],
                          S["ygt"][:, :])
            ch_yg.wait("gpsimd")
            yga2a.append(ctx.inc("gpsimd", "cc", 1,
                                 lambda e, c=c: e.collective_compute(
                                     "AllGather", OP.bypass, replica_groups=PAIRS,
                                     ins=[T["yg_w%d" % c].opt()],
                                     outs=[T["yg_e%d" % c].opt()])))
        return yga2a

    def load_outw():
        ctx.full_barrier()
        frot = Rot(ctx, 2)
        sv = None
        for ki in range(G_DI):
            for co, cw_ in sub_cols(D):
                j = frot.acquire("sync")
                wf = S["wf32a" if j == 0 else "wf32b"]
                frot.load(j, "sync", wf[:, 0:cw_],
                          T["outwT"][ki * 128:(ki + 1) * 128, co:co + cw_])
                frot.wait_ch(j, "scalar")
                sv = ctx.inc("scalar", "s", 1,
                             lambda e, ki=ki, co=co, w=cw_, wf=wf: e.copy(
                                 S["owb"][:, ki * D + co:ki * D + co + w],
                                 wf[:, 0:w]))
                frot.release(j, "s", sv)
        return sv

    def phase_p5b(yga2a, owv):
        ctx.full_barrier()
        ctx.wait("tensor", "s", owv)
        pmrot = Rot(ctx, 2, chan=False)
        strot = Rot(ctx, 2)
        grot = Rot(ctx, 2)
        ch_out = Chan(ctx)
        ch_xh = Chan(ctx)
        for ci in range(4):
            t0 = ci * 512
            ctx.wait("sync", "cc", yga2a[ci + 4])
            ctx.barrier_from("sync", "t")
            for ki in range(G_DI):
                gj = grot.acquire("sync")
                ga = S["bf0" if gj == 0 else "bf1"]
                gb = S["hbf0" if gj == 0 else "hbf1"]
                grot.load(gj, "sync", ga[:, 0:512],
                          T["yg_e%d" % ci][ki * 128:(ki + 1) * 128, 0:512])
                grot.load(gj, "sync", gb[:, 0:512],
                          T["yg_e%d" % (ci + 4)][ki * 128:(ki + 1) * 128, 0:512])
                grot.wait_ch(gj, "vector")
                ctx.op("vector", lambda e, ki=ki, ga=ga: e.tensor_scalar_mul(
                    S["ygc"][:, ki * 512:(ki + 1) * 512],
                    ga[:, 0:512], S["fl0"][:, 0:1]))
                vvv = ctx.inc("vector", "v", 1,
                              lambda e, ki=ki, gb=gb: e.scalar_tensor_tensor(
                                  S["ygc"][:, ki * 512:(ki + 1) * 512],
                                  gb[:, 0:512], S["fl1"][:, 0:1],
                                  S["ygc"][:, ki * 512:(ki + 1) * 512],
                                  OP.mult, OP.add))
                grot.release(gj, "v", vvv)
            ctx.wait("tensor", "v", ctx.val("v"))
            tlast_stats = None
            for dm in range(G_D):
                pj = pmrot.acquire("tensor")
                pm = P["pm0" if pj == 0 else "pm1"]
                tv = None
                for ki in range(G_DI):
                    last = ki == G_DI - 1
                    fn = lambda e, pm=pm, ki=ki, dm=dm, last=last: e.matmul(
                        pm[:, 0:512],
                        S["owb"][:, ki * D + dm * 128:ki * D + (dm + 1) * 128],
                        S["ygc"][:, ki * 512:(ki + 1) * 512],
                        start=(ki == 0), stop=last)
                    if last:
                        tv = ctx.inc("tensor", "t", 1, fn)
                    else:
                        ctx.op("tensor", fn)
                si = strot.acquire("sync")
                sta = S["stA0" if si == 0 else "stA1"]
                stb = S["stB0" if si == 0 else "stB1"]
                strot.load(si, "sync", stb[:, 0:512],
                           T["x1"][dm * 128:(dm + 1) * 128,
                                   HALO + t0:HALO + t0 + 512])
                ctx.wait("vector", "t", tv)
                strot.wait_ch(si, "vector")
                vv = ctx.inc("vector", "v", 1,
                             lambda e, pm=pm, sta=sta, stb=stb: e.tensor_add(
                                 sta[:, 0:512], pm[:, 0:512], stb[:, 0:512]))
                pmrot.release(pj, "v", vv)
                ctx.wait("scalar", "v", vv)
                ch_out.dma("scalar", T["x2"][dm * 128:(dm + 1) * 128, t0:t0 + 512],
                           sta[:, 0:512])
                sv = ctx.inc("scalar", "s", 1, lambda e, sta=sta: e.activation(
                    S["hsq0"][:, 0:512], sta[:, 0:512], AF.Square))
                ctx.wait("tensor", "s", sv)
                tlast_stats = ln_stats_mm(dm, G_D, 512, sta[:, 0:512],
                                          S["hsq0"][:, 0:512])
                strot.release(si, "t", tlast_stats)
                strot.release(si, ch_out.name, ctx.val(ch_out.name))
            ch_out.wait("sync")
            bvv = ln_finish(512, D, tlast_stats)
            ctx.wait("vector", "s", bvv)
            for dm in range(G_D):
                si = strot.acquire("sync")
                sta = S["stA0" if si == 0 else "stA1"]
                strot.load(si, "sync", sta[:, 0:512],
                           T["x2"][dm * 128:(dm + 1) * 128, t0:t0 + 512])
                strot.wait_ch(si, "vector")
                ctx.op("vector", lambda e, sta=sta: e.tensor_sub(
                    sta[:, 0:512], sta[:, 0:512], S["ln_mB"][:, 0:512]))
                ctx.op("vector", lambda e, sta=sta: e.tensor_mul(
                    sta[:, 0:512], sta[:, 0:512], S["ln_iB"][:, 0:512]))
                vv = ctx.inc("vector", "v", 1,
                             lambda e, sta=sta, j=ci * G_D + dm: e.tensor_reduce(
                                 S["amaxcols"][:, j:j + 1], sta[:, 0:512], AX.X,
                                 OP.max, apply_absolute_value=True))
                ctx.wait("scalar", "v", vv)
                wv = ch_xh.dma("scalar",
                               T["xhat3"][dm * 128:(dm + 1) * 128, t0:t0 + 512],
                               sta[:, 0:512])
                strot.release(si, "v", vv)
                strot.release(si, ch_xh.name, wv)
        return amax_finish(0, 4 * G_D, S["sc3"][0:1, 0:1], ctx.val("v"))

    phase_w()
    phase_p1()
    allreduce_max(1)
    mk_scale(S["gsc1"][0:1, 0:1], "qx1")
    mk_scale(S["gsc1"][0:1, 1:2], "qwfc")
    mk_scale(S["gsc1"][0:1, 2:3], "qwpj")
    mk_dq(S["gsc1"][0:1, 0:1], S["gsc1"][0:1, 1:2], 1.0, "dq1")
    phase_fc(T["xhat1"], T["h1"], TP, TOFF, TCH, True, "qx1", "dq1", "sc2")
    allreduce_max(2)
    mk_scale(S["gsc2"][0:1, 0:1], "qh1")
    mk_dq(S["gsc2"][0:1, 0:1], S["gsc1"][0:1, 2:3], 0.5, "dq2")
    wr = load_wproj(True)
    phase_proj(T["h1"], T["x0t"], T["x1"], TOFF, TCH, "qh1", "dq2", wr,
               ln_xhat=T["xhat2"])
    wxv = prep_win_bf()
    a2a, pg = phase_p4(wxv)
    ldt_done = prep_ldt(pg)
    ow = load_outw()
    yga = phase_p5(a2a, ldt_done)
    phase_p5b(yga, ow)
    allreduce_max(3)
    mk_scale(S["gsc3"][0:1, 0:1], "qx3")
    mk_dq(S["gsc3"][0:1, 0:1], S["gsc1"][0:1, 1:2], 1.0, "dq3")
    phase_fc(T["xhat3"], T["h2"], TL, TOFF2, TCH2, False, "qx3", "dq3", "sc4")
    allreduce_max(4)
    mk_scale(S["gsc4"][0:1, 0:1], "qh2")
    mk_dq(S["gsc4"][0:1, 0:1], S["gsc1"][0:1, 2:3], 0.5, "dq4")
    wr = load_wproj(False)
    phase_proj(T["h2"], T["x2"], T["out"], TOFF2, TCH2, "qh2", "dq4", wr,
               ln_xhat=None)
    ctx.full_barrier()
    if "dbg_sc" in DEBUG_OUTS:
        for j, nm in enumerate(("gsc1", "gsc2", "gsc3", "gsc4",
                                "sc1", "sc2", "sc3", "sc4")):
            ctx.dma("gpsimd", T["dbg_sc"][0:1, j * 4:j * 4 + 4],
                    S[nm][0:1, 0:4], "qg")
        for j, nm in enumerate(("qx1", "qh1", "qx3", "qh2", "qwfc", "qwpj",
                                "dq1", "dq2", "dq3", "dq4")):
            ctx.dma("gpsimd", T["dbg_sc"][0:1, 32 + j:33 + j],
                    S[nm][0:1, 0:1], "qg")
        ctx.barrier_from("gpsimd", "qg")

    with nc.Block() as block:
        @block.sync
        def _(eng):
            for f in ctx.ops["sync"]:
                f(eng)

        @block.scalar
        def _(eng):
            for f in ctx.ops["scalar"]:
                f(eng)

        @block.vector
        def _(eng):
            for f in ctx.ops["vector"]:
                f(eng)

        @block.tensor
        def _(eng):
            for f in ctx.ops["tensor"]:
                f(eng)

        @block.gpsimd
        def _(eng):
            for f in ctx.ops["gpsimd"]:
                f(eng)

    assert nc.sbuf_base <= base, (nc.sbuf_base, base)
    cs.__exit__(None, None, None)
    return nc


_NC_CACHE = {}


def kernel(L0_attn_x, L0_attn_W_in, L0_attn_conv_w, L0_attn_conv_b, L0_attn_W_x,
           L0_attn_A_log, L0_attn_dt_bias, L0_attn_out_W, L0_attn_D,
           L0_ffn_w_fc, L0_ffn_w_proj):
    x0 = np.asarray(L0_attn_x, dtype=np.float32)
    winT = np.ascontiguousarray(np.asarray(L0_attn_W_in, np.float32).T)
    wxT = np.ascontiguousarray(np.asarray(L0_attn_W_x, np.float32).T)
    outwT = np.ascontiguousarray(np.asarray(L0_attn_out_W, np.float32).T)
    wfcT = np.ascontiguousarray(np.asarray(L0_ffn_w_fc, np.float32).T)
    wprojT = np.ascontiguousarray(np.asarray(L0_ffn_w_proj, np.float32).T)
    convw = np.ascontiguousarray(np.asarray(L0_attn_conv_w, np.float32)[:, 0, :])
    convb = np.ascontiguousarray(np.asarray(L0_attn_conv_b, np.float32)[None, :])
    dtb = np.ascontiguousarray(np.asarray(L0_attn_dt_bias, np.float32)[None, :])
    alog = np.ascontiguousarray(np.asarray(L0_attn_A_log, np.float32))
    dvec = np.ascontiguousarray(np.asarray(L0_attn_D, np.float32)[None, :])

    if "nc" not in _NC_CACHE:
        _NC_CACHE["nc"] = build_program()
    nc = _NC_CACHE["nc"]

    in_maps = []
    for c in range(8):
        b, h = c // 2, c % 2
        x_slice = x0[b]
        xt = np.zeros((D, TP), np.float32)
        if h == 0:
            xt[:, HALO:] = x_slice[0:TL, :].T
        else:
            xt[:, :] = x_slice[TL - HALO:, :].T
        in_maps.append({
            "x0t": np.ascontiguousarray(xt),
            "winT": winT, "wxT": wxT, "outwT": outwT,
            "wfcT": wfcT, "wprojT": wprojT,
            "wfc_sl": np.ascontiguousarray(wfcT[c * (D // 8):(c + 1) * (D // 8), :]),
            "wproj_sl": np.ascontiguousarray(wprojT[c * (FF // 8):(c + 1) * (FF // 8), :]),
            "a_log_h": np.ascontiguousarray(alog[h * DH:(h + 1) * DH, :]),
            "d_h": np.ascontiguousarray(dvec[:, h * DH:(h + 1) * DH]),
            "conv_w": convw, "conv_b": convb, "dt_bias": dtb,
            "flags": np.array([[1.0 - h, float(h)]], np.float32),
        })
    res = run_bass_kernel_spmd(nc, in_maps, core_ids=list(range(8)))
    out = np.empty((4, SQ, D), np.float32)
    for c in range(8):
        b, h = c // 2, c % 2
        out[b, h * TL:(h + 1) * TL, :] = np.asarray(res.results[c]["out"]).T
    return out
